# version 02
# baseline (speedup 1.0000x reference)
import math
from contextlib import ExitStack

import numpy as np
import ml_dtypes

import concourse.bass as bass
import concourse.mybir as mybir
from concourse.bass_utils import run_bass_kernel_spmd

F32 = mybir.dt.float32
BF16 = mybir.dt.bfloat16
AF = mybir.ActivationFunctionType
ALU = mybir.AluOpType
AX = mybir.AxisListType

D = 2048
HD = 1408
NE = 8
NH = 16
NT = 2048
NG = 4
NDC = 16
NHC = 11
NKB = 32
ROWS_EVEN = [0, 3, 4, 7, 8, 11, 12, 15]
ROWS_ODD = [1, 2, 5, 6, 9, 10, 13, 14]
L_PROG = [4, 8, 12, 16, 20, 24, 28, 32]
INV_SQRT_HD = 1.0 / math.sqrt(128.0)


class Ctr:
    def __init__(self, handle, name):
        self.h = handle
        self.v = 0
        self.name = name


class Builder:
    ENGINES = ("tensor", "vector", "scalar", "sync", "gpsimd")

    def __init__(self, nc):
        self.nc = nc
        self.q = {e: [] for e in self.ENGINES}
        self.waited = {e: {} for e in self.ENGINES}
        self.uid = 0

    def op(self, eng, f, inc=None, incn=1, waits=()):
        pruned = []
        w = self.waited[eng]
        for item in waits:
            if item is None:
                continue
            s, v = item
            if v is None or v <= 0:
                continue
            if w.get(s.name, 0) >= v:
                continue
            w[s.name] = v
            pruned.append((s, v))
        after = None
        if inc is not None:
            inc.v += incn
            after = inc.v

        def run(e, f=f, inc=inc, incn=incn, pruned=tuple(pruned)):
            for (s, v) in pruned:
                e.wait_ge(s.h, v)
            ins = f(e)
            if inc is not None:
                ins.then_inc(inc.h, incn)

        self.q[eng].append(run)
        return after

    def wait(self, eng, waits):
        self.op(eng, lambda e: None, waits=waits)

    def emit(self, block):
        for eng in self.ENGINES:
            fns = self.q[eng]

            def body(e, fns=fns):
                for fn in fns:
                    fn(e)

            getattr(block, eng)(body)


def build_program(mode="full"):
    nc = bass.Bass("TRN2", target_bir_lowering=False)

    xT = nc.dram_tensor("xT", [D, NT], F32, kind="ExternalInput").ap()
    wqkvT = nc.dram_tensor("wqkvT", [18, 128, 2048], F32, kind="ExternalInput").ap()
    qkvb_d = nc.dram_tensor("qkvb", [128, 18], F32, kind="ExternalInput").ap()
    woT = nc.dram_tensor("woT", [NDC, 128, 2048], F32, kind="ExternalInput").ap()
    outb_d = nc.dram_tensor("outb", [128, 16], F32, kind="ExternalInput").ap()
    gw_d = nc.dram_tensor("gw", [8, D], F32, kind="ExternalInput").ap()
    gb_d = nc.dram_tensor("gb", [1, 8], F32, kind="ExternalInput").ap()
    gbias_d = nc.dram_tensor("gbias", [1, 8], F32, kind="ExternalInput").ap()
    w1t = nc.dram_tensor("w1t", [NE, 128, NDC * HD], F32, kind="ExternalInput").ap()
    b1c_d = nc.dram_tensor("b1c", [128, NE * NHC], F32, kind="ExternalInput").ap()
    w2t = nc.dram_tensor("w2t", [NE, 128, NHC * D], F32, kind="ExternalInput").ap()
    b2_d = nc.dram_tensor("b2", [8, D], F32, kind="ExternalInput").ap()
    masks_d = nc.dram_tensor("masks", [128, 8 * 4 * 256], BF16, kind="ExternalInput").ap()
    sel_d = nc.dram_tensor("sel", [8, NE * 128], BF16, kind="ExternalInput").ap()
    y = nc.dram_tensor("y", [D, NT], F32, kind="ExternalOutput").ap()
    dbg = {}
    dbg2 = {}
    dump_pre = {"attn_dbgq": ("qT",), "attn_dbgk": ("kT", "vaug"), "attn_dbgl": ("ln2T",)}.get(mode, ())
    if dump_pre:
        for nm, shp in (("kT", [128, 4096]), ("vaug", [128, NKB * 132]),
                        ("qT", [128, NH * NT]), ("ln2T", [128, NDC * NT])):
            if nm in dump_pre:
                dbg[nm] = nc.dram_tensor(f"dbg_{nm}", shp, F32, kind="ExternalOutput").ap()
        dbg2["ctxT"] = nc.dram_tensor("dbg_ctxT", [128, NH * NT], F32, kind="ExternalOutput").ap()
    if mode in ("attn_dbg2", "full_dbg"):
        dbg2["ctxT"] = nc.dram_tensor("dbg_ctxT", [128, NH * NT], F32, kind="ExternalOutput").ap()
        dbg2["qT_post"] = nc.dram_tensor("dbg_qT_post", [128, NH * NT], F32, kind="ExternalOutput").ap()
        dbg2["kT_post"] = nc.dram_tensor("dbg_kT_post", [128, 4096], F32, kind="ExternalOutput").ap()
        dbg2["vaug_post"] = nc.dram_tensor("dbg_vaug_post", [128, NKB * 132], F32, kind="ExternalOutput").ap()
        dbg2["ln2T_post"] = nc.dram_tensor("dbg_ln2T_post", [128, NDC * NT], F32, kind="ExternalOutput").ap()
        dbg2["rows"] = nc.dram_tensor("dbg_rows", [1, 2048], F32, kind="ExternalOutput").ap()
        dbg2["psm"] = nc.dram_tensor("dbg_psm", [1, 1024], F32, kind="ExternalOutput").ap()
        dbg2["xload3d"] = nc.dram_tensor("dbg_xload3d", [128, NDC * 128], F32, kind="ExternalOutput").ap()
        dbg2["xload2d"] = nc.dram_tensor("dbg_xload2d", [128, 2 * 128], F32, kind="ExternalOutput").ap()
    if mode == "attn_dbg":
        dbg["kT"] = nc.dram_tensor("dbg_kT", [128, 4096], F32, kind="ExternalOutput").ap()
        dbg["vaug"] = nc.dram_tensor("dbg_vaug", [128, NKB * 132], F32, kind="ExternalOutput").ap()
        dbg["qT"] = nc.dram_tensor("dbg_qT", [128, NH * NT], F32, kind="ExternalOutput").ap()
        dbg["ctxT"] = nc.dram_tensor("dbg_ctxT", [128, NH * NT], F32, kind="ExternalOutput").ap()
        dbg["ln2T"] = nc.dram_tensor("dbg_ln2T", [128, NDC * NT], F32, kind="ExternalOutput").ap()

    xn_sc = nc.dram_tensor("xn_sc", [D, NT], BF16).ap()
    _kind = "ExternalOutput" if mode == "full_dbg" else "Internal"
    x2_sc = nc.dram_tensor("x2_sc", [D, NT], F32, kind=_kind).ap()
    x3_sc = nc.dram_tensor("x3_sc", [D, NT], F32, kind=_kind).ap()
    cc_in = nc.dram_tensor("cc_in", [256, NT], BF16).ap()
    cc_out = nc.dram_tensor("cc_out", [512, NT], BF16).ap()

    es = ExitStack()

    def sem(name):
        return Ctr(es.enter_context(nc.semaphore(name)), name)

    cW = sem("cW")
    cA = sem("cA")
    cS = sem("cS")
    cP = sem("cP")
    cV = sem("cV")
    cC = sem("cC")
    cG = sem("cG")
    cCC = sem("cCC")
    ALLC = (cW, cA, cS, cP, cV, cC, cG)

    bld = Builder(nc)

    def join_all():
        vals = [(c, c.v) for c in ALLC]
        for eng in Builder.ENGINES:
            bld.wait(eng, vals)

    def drain_barrier(n):
        for _ in range(n):
            bld.op("vector", lambda v: v.tensor_copy(drain_sb[:, :], ident_f[:, :]),
                   inc=cV, waits=[(cS, cS.v)])
            bld.op("sync", lambda e: e.dma_start(out=drain_sc, in_=drain_sb[:, :]),
                   inc=cS, incn=16, waits=[(cV, cV.v)])
        join_all()

    import os as _os
    _dbg_alloc = _os.environ.get("MK_DEBUG_ALLOC")

    def sb(scope, name, shape, dt, side=None):
        bld.uid += 1
        if side:
            h = scope.enter_context(nc.sbuf_tensor(f"{name}_{bld.uid}", shape, dt, side=side))
        else:
            h = scope.enter_context(nc.sbuf_tensor(f"{name}_{bld.uid}", shape, dt))
        if _dbg_alloc:
            import concourse.mybir as _mb
            per = 1
            for s_ in shape[1:]:
                per *= s_
            per *= _mb.dt.size(dt)
            print(f"sb {name}_{bld.uid} {shape} addr={nc.lookup_mloc(h).addr} end={nc.lookup_mloc(h).addr+per} rem={nc.sbuf_bytes_remaining}")
        return h

    def ps(scope, name, shape, dt):
        bld.uid += 1
        return scope.enter_context(nc.psum_tensor(f"{name}_{bld.uid}", shape, dt))

    ident_f = sb(es, "ident_f", [128, 128], F32)
    ident_b = sb(es, "ident_b", [128, 128], BF16)
    ones_b = sb(es, "ones_b", [1, 128], BF16)
    qkvb_sb = sb(es, "qkvb_sb", [128, 18], F32)
    outb_sb = sb(es, "outb_sb", [128, 16], F32)
    b1c_sb = sb(es, "b1c_sb", [128, NE * NHC], F32)
    b2_sb = sb(es, "b2_sb", [8, D], BF16)
    gwT_sb = sb(es, "gwT_sb", [128, NDC * 8], BF16)
    gbb_f = sb(es, "gbb_f", [1, 8], F32)
    gbb_sb = sb(es, "gbb_sb", [1, 8], BF16)
    gbt_f = sb(es, "gbt_f", [1, 8], F32)
    combT_sb = sb(es, "combT_sb", [8, NT], BF16)
    sel_sb = sb(es, "sel_sb", [8, NE * 128], BF16)
    eps_col = sb(es, "eps_col", [128, 1], F32)
    drain_sb = sb(es, "drain_sb", [128, 128], F32)
    drain_sc = nc.dram_tensor("drain_sc", [128, 128], F32).ap()

    init_scope = ExitStack()
    gw_sb = sb(init_scope, "gw_sb", [8, D], BF16)
    with nc.psum_tensor("ps_init", [128, 8], BF16) as ps_init:
        bld.op("gpsimd", lambda g: g.memset(ident_f[:, :], 0.0))
        bld.op("gpsimd", lambda g: g.affine_select(
            out=ident_f[:, :], in_=ident_f[:, :], compare_op=ALU.not_equal,
            fill=1.0, base=0, pattern=[[-1, 128]], channel_multiplier=1), inc=cG)
        bld.op("gpsimd", lambda g: g.memset(ident_b[:, :], 0.0))
        bld.op("gpsimd", lambda g: g.affine_select(
            out=ident_b[:, :], in_=ident_b[:, :], compare_op=ALU.not_equal,
            fill=1.0, base=0, pattern=[[-1, 128]], channel_multiplier=1), inc=cG)
        bld.op("gpsimd", lambda g: g.memset(ones_b[:, :], 1.0), inc=cG)
        v_init_g = cG.v
        bld.op("sync", lambda e: e.dma_start(out=qkvb_sb[:, :], in_=qkvb_d), inc=cA, incn=16)
        bld.op("sync", lambda e: e.dma_start(out=outb_sb[:, :], in_=outb_d), inc=cA, incn=16)
        bld.op("sync", lambda e: e.dma_start(out=b1c_sb[:, :], in_=b1c_d), inc=cA, incn=16)
        bld.op("sync", lambda e: e.dma_start(out=gbt_f[:, :], in_=gb_d), inc=cA, incn=16)
        bld.op("sync", lambda e: e.dma_start(out=gbb_f[:, :], in_=gbias_d), inc=cA, incn=16)
        bld.op("sync", lambda e: e.dma_start(out=sel_sb[:, :], in_=sel_d), inc=cA, incn=16)
        v_init_a = cA.v
        bld.op("gpsimd", lambda g: g.dma_start(out=b2_sb[:, :], in_=b2_d), inc=cW, incn=16)
        bld.op("gpsimd", lambda g: g.dma_start(out=gw_sb[:, :], in_=gw_d), inc=cW, incn=16)
        v_init_w = cW.v
        bld.op("vector", lambda v: v.memset(eps_col[:, :], 1e-5), inc=cV)
        bld.op("vector", lambda v: v.tensor_add(gbb_f[:, :], gbb_f[:, :], gbt_f[:, :]),
               inc=cV, waits=[(cA, v_init_a)])
        bld.op("vector", lambda v: v.tensor_copy(gbb_sb[:, :], gbb_f[:, :]),
               inc=cV, waits=[(cV, cV.v)])
        v_gbb = cV.v
        for dc in range(NDC):
            bld.op("tensor", lambda t, dc=dc: t.transpose(
                ps_init[:, :], gw_sb[0:8, dc * 128:(dc + 1) * 128], ident_b[0:8, 0:8]),
                inc=cP, waits=[(cW, v_init_w), (cG, v_init_g), (cV, cV.v)])
            bld.op("vector", lambda v, dc=dc: v.tensor_copy(
                gwT_sb[:, dc * 8:(dc + 1) * 8], ps_init[:, :]),
                inc=cV, waits=[(cP, cP.v)])
        v_gwT = cV.v
    init_scope.close()

    init_done = {"g": v_init_g, "a": v_init_a, "w": v_init_w,
                 "gbb": v_gbb, "gwT": v_gwT}

    lnst = {}

    def alloc_ln_staging(scope, nbuf=2):
        lnst.clear()
        xs = [sb(scope, "xstg", [128, NDC * 128], F32) for _ in range(nbuf)]
        xt = [sb(scope, "xtok", [128, D], F32) for _ in range(nbuf)]
        lnst["xstage"] = [xs[i % nbuf] for i in range(2)]
        lnst["xtok"] = [xt[i % nbuf] for i in range(2)]
        lnst["xsq"] = sb(scope, "xsq", [128, D], BF16)
        lnst["stat"] = sb(scope, "stat", [128, 8], F32)
        lnst["xn_tok"] = [sb(scope, "xntok", [128, D], BF16) for _ in range(2)]
        lnst["ps_ti"] = [ps(scope, "ps_ti", [128, 512], F32) for _ in range(2)]
        lnst["ps_to"] = [ps(scope, "ps_to", [128, 1024], BF16) for _ in range(2)]
        lnst["busy"] = {}

    def emit_ln_tile(x_src, t, dst_sb, stride, col0):
        xs = lnst["xstage"][t % 2]
        xt = lnst["xtok"][t % 2]
        xn = lnst["xn_tok"][t % 2]
        stat = lnst["stat"]
        busy = lnst["busy"]
        src = x_src.rearrange("(dc p) t -> p dc t", p=128)[:, :, t * 128:(t + 1) * 128]
        bld.op("sync", lambda e, xs=xs, src=src: e.dma_start(
            out=xs[:, :].rearrange("p (dc t) -> p dc t", t=128), in_=src),
            inc=cA, incn=16, waits=[busy.get(("xs", t % 2))])
        vload = cA.v
        for q in range(4):
            pb = lnst["ps_ti"][q % 2]
            for j in range(4):
                dc = q * 4 + j
                wts = [(cA, vload), busy.get(("ti", q % 2))] if j == 0 else []
                bld.op("tensor", lambda t_, pb=pb, j=j, xs=xs, dc=dc: t_.transpose(
                    pb[:, j * 128:(j + 1) * 128], xs[:, dc * 128:(dc + 1) * 128],
                    ident_f[:, :]), inc=(cP if j == 3 else None), waits=wts)
            vtr = cP.v
            bld.op("vector", lambda v, xt=xt, q=q, pb=pb: v.tensor_copy(
                xt[:, q * 512:(q + 1) * 512], pb[:, :]), inc=cV, waits=[(cP, vtr)])
            busy[("ti", q % 2)] = (cV, cV.v)
        vxt = cV.v
        busy[("xs", t % 2)] = (cP, vtr)
        bld.op("vector", lambda v, xt=xt: v.tensor_reduce(
            stat[:, 0:1], xt[:, :], AX.X, ALU.add, negate=True),
            inc=cV, waits=[(cV, vxt)])
        bld.op("vector", lambda v: v.tensor_scalar(
            stat[:, 1:2], stat[:, 0:1], 1.0 / D, 0.0, ALU.mult, ALU.add),
            inc=cV, waits=[(cV, cV.v)])
        v_negm = cV.v
        bld.op("scalar", lambda s, xt=xt: s.activation(
            lnst["xsq"][:, :], xt[:, :], AF.Square, accum_out=stat[:, 2:3]),
            inc=cC, waits=[(cV, vxt)])
        v_ssq = cC.v
        bld.op("vector", lambda v: v.tensor_mul(
            stat[:, 3:4], stat[:, 1:2], stat[:, 1:2]), inc=cV, waits=[(cV, v_negm)])
        bld.op("vector", lambda v: v.scalar_tensor_tensor(
            stat[:, 4:5], stat[:, 2:3], 1.0 / D, stat[:, 3:4],
            op0=ALU.mult, op1=ALU.subtract), inc=cV,
            waits=[(cV, cV.v), (cC, v_ssq)])
        v_var = cV.v
        bld.op("scalar", lambda s: s.activation(
            stat[:, 5:6], stat[:, 4:5], AF.Ln, bias=eps_col[:, :]),
            inc=cC, waits=[(cV, v_var)])
        bld.op("scalar", lambda s: s.activation(
            stat[:, 6:7], stat[:, 5:6], AF.Exp, scale=-0.5), inc=cC, waits=[(cC, cC.v)])
        v_rstd = cC.v
        bld.op("vector", lambda v, xt=xt, xn=xn: v.tensor_scalar(
            xn[:, :], xt[:, :], stat[:, 1:2], stat[:, 6:7], ALU.add, ALU.mult),
            inc=cV, waits=[(cC, v_rstd)])
        v_xn = cV.v
        ev_last = None
        for q in range(4):
            pb = lnst["ps_to"][q % 2]
            for j in range(4):
                dc = q * 4 + j
                wts = [(cV, v_xn), busy.get(("to", q % 2))] if j == 0 else []
                bld.op("tensor", lambda t_, pb=pb, j=j, xn=xn, dc=dc: t_.transpose(
                    pb[:, j * 128:(j + 1) * 128], xn[:, dc * 128:(dc + 1) * 128],
                    ident_b[:, :]), inc=(cP if j == 3 else None), waits=wts)
            vtr = cP.v

            for j in range(4):
                dc = q * 4 + j

                def ev_fn(s, pb=pb, j=j, dc=dc):
                    return s.copy(
                        dst_sb[:, dc * stride + col0: dc * stride + col0 + 128],
                        pb[:, j * 128:(j + 1) * 128])
                bld.op("scalar", ev_fn, inc=(cC if j == 3 else None),
                       waits=([(cP, vtr)] if j == 0 else []))
            busy[("to", q % 2)] = (cC, cC.v)
            ev_last = cC.v
        return ev_last, vtr

    ffn_state = {"w1_free": None, "w2_free": None}

    def emit_ffn_call(x_src, x_dst, tag):
        join_all()
        call = ExitStack()
        w1_sb = sb(call, f"w1{tag}", [128, NDC * HD], BF16)
        w2_sb = sb(call, f"w2{tag}", [128, NHC * D], BF16)

        def load_w1(e_idx, waits):
            return bld.op("gpsimd", lambda g, e_idx=e_idx: g.dma_start(
                out=w1_sb[:, :], in_=w1t[e_idx]), inc=cW, incn=16, waits=waits)

        def load_w2(e_idx, waits):
            return bld.op("gpsimd", lambda g, e_idx=e_idx: g.dma_start(
                out=w2_sb[:, :], in_=w2t[e_idx]), inc=cW, incn=16, waits=waits)

        pa = ExitStack()
        alloc_ln_staging(pa)
        xnT_stage = [sb(pa, f"xnstg{tag}", [128, NDC * 512], BF16) for _ in range(2)]
        sgate = sb(pa, f"sgate{tag}", [128, 8], F32)
        gtmp = sb(pa, f"gtmp{tag}", [128, 48], F32)
        comb_bf = sb(pa, f"combbf{tag}", [128, 8], BF16)
        ps_gate = ps(pa, f"psg{tag}", [128, 512], F32)
        ps_ct = ps(pa, f"psct{tag}", [128, 1024], BF16)

        v_w1 = load_w1(0, [ffn_state["w1_free"]])
        v_w2 = load_w2(0, [ffn_state["w2_free"]])

        stage_store = [None, None]
        stage_busy = [None, None]
        gate_free = None
        ct_free = None
        comb_done_g = []
        xnT_store_g = []
        for g in range(NG):
            stg = xnT_stage[g % 2]
            pre = [w for w in (
                ((cS, stage_store[g % 2]) if stage_store[g % 2] else None),
                stage_busy[g % 2]) if w]
            if pre:
                bld.wait("scalar", pre)
                bld.wait("sync", pre)
            for tl in range(4):
                t = g * 4 + tl
                last_ev, _ = emit_ln_tile(x_src, t, stg, 512, tl * 128)
                wts = [(cC, last_ev), (cV, init_done["gwT"])]
                if gate_free is not None:
                    wts.append((cV, gate_free))
                for dc in range(NDC):
                    bld.op("tensor", lambda t_, stg=stg, dc=dc, tl=tl: t_.matmul(
                        ps_gate[:, 0:8],
                        stg[:, dc * 512 + tl * 128: dc * 512 + (tl + 1) * 128],
                        gwT_sb[:, dc * 8:(dc + 1) * 8],
                        start=(dc == 0), stop=False), waits=wts)
                    wts = []
                bld.op("tensor", lambda t_: t_.matmul(
                    ps_gate[:, 0:8], ones_b[:, :], gbb_sb[:, :],
                    start=False, stop=True), inc=cP,
                    waits=[(cV, init_done["gbb"]), (cG, init_done["g"])])
                v_gmm = cP.v
                bld.op("vector", lambda v: v.tensor_copy(sgate[:, :], ps_gate[:, 0:8]),
                       inc=cV, waits=[(cP, v_gmm)])
                gate_free = cV.v
                bld.op("vector", lambda v: v.tensor_reduce(
                    gtmp[:, 0:1], sgate[:, :], AX.X, ALU.max, negate=True),
                    inc=cV, waits=[(cV, cV.v)])
                v_m1 = cV.v
                bld.op("vector", lambda v: v.tensor_scalar(
                    gtmp[:, 8:16], sgate[:, :], gtmp[:, 0:1], 0.0, ALU.add, ALU.is_ge),
                    inc=cV, waits=[(cV, v_m1)])
                bld.op("vector", lambda v: v.scalar_tensor_tensor(
                    gtmp[:, 16:24], gtmp[:, 8:16], -1e30, sgate[:, :],
                    op0=ALU.mult, op1=ALU.add), inc=cV, waits=[(cV, cV.v)])
                bld.op("vector", lambda v: v.tensor_reduce(
                    gtmp[:, 1:2], gtmp[:, 16:24], AX.X, ALU.max, negate=True),
                    inc=cV, waits=[(cV, cV.v)])
                bld.op("vector", lambda v: v.tensor_scalar(
                    gtmp[:, 24:32], sgate[:, :], gtmp[:, 1:2], 0.0, ALU.add, ALU.is_ge),
                    inc=cV, waits=[(cV, cV.v)])
                v_mask2 = cV.v
                bld.op("scalar", lambda s: s.activation(
                    gtmp[:, 32:40], sgate[:, :], AF.Exp, bias=gtmp[:, 0:1]),
                    inc=cC, waits=[(cV, v_m1)])
                v_z = cC.v
                bld.op("vector", lambda v: v.tensor_mul(
                    gtmp[:, 32:40], gtmp[:, 32:40], gtmp[:, 24:32]),
                    inc=cV, waits=[(cV, v_mask2), (cC, v_z)])
                bld.op("vector", lambda v: v.tensor_reduce(
                    gtmp[:, 2:3], gtmp[:, 32:40], AX.X, ALU.add),
                    inc=cV, waits=[(cV, cV.v)])
                bld.op("vector", lambda v: v.reciprocal(
                    gtmp[:, 3:4], gtmp[:, 2:3]), inc=cV, waits=[(cV, cV.v)])
                bld.op("vector", lambda v: v.tensor_scalar(
                    comb_bf[:, :], gtmp[:, 32:40], gtmp[:, 3:4], 0.5,
                    ALU.mult, ALU.mult), inc=cV, waits=[(cV, cV.v)])
                v_comb = cV.v
                wts = [(cV, v_comb)]
                if ct_free is not None:
                    wts.append((cV, ct_free))
                bld.op("tensor", lambda t_: t_.transpose(
                    ps_ct[0:8, 0:128], comb_bf[:, :], ident_b[:, :]),
                    inc=cP, waits=wts)
                v_ctt = cP.v
                bld.op("vector", lambda v, g=g, tl=tl: v.tensor_copy(
                    combT_sb[0:8, g * 512 + tl * 128: g * 512 + (tl + 1) * 128],
                    ps_ct[0:8, 0:128]), inc=cV, waits=[(cP, v_ctt)])
                ct_free = cV.v
                stage_busy[g % 2] = (cP, v_gmm)
            comb_done_g.append(cV.v)
            bld.op("sync", lambda e, stg=stg, g=g: e.dma_start(
                out=xn_sc.rearrange("(dc p) t -> p dc t", p=128)[:, :, g * 512:(g + 1) * 512],
                in_=stg[:, :].rearrange("p (dc t) -> p dc t", t=512)),
                inc=cS, incn=16, waits=[(cC, last_ev)])
            stage_store[g % 2] = cS.v
            xnT_store_g.append(cS.v)
        pa.close()

        join_all()
        pb_s = ExitStack()
        xg_sb = [sb(pb_s, f"xg{tag}", [128, NDC * 512], F32) for _ in range(2)]
        xnTg_sb = [sb(pb_s, f"xnTg{tag}", [128, NDC * 512], BF16) for _ in range(2)]
        h_sb = sb(pb_s, f"h{tag}", [128, NHC * 512], BF16)
        ps_h = [ps(pb_s, f"psh{tag}", [128, 512], F32) for _ in range(2)]
        ps_y = [ps(pb_s, f"psy{tag}", [128, 512], F32) for _ in range(2)]
        ps_bc = ps(pb_s, f"psbc{tag}", [128, 512], F32)

        xg_store = [None, None]
        xg_busy = [None, None]
        xnTg_busy = [None, None]
        silu_done = {}
        mult_done = {}
        evict_done = {}
        bcast_free = None
        v_f1_last = None
        v_f2_last = None
        for g in range(NG):
            xg = xg_sb[g % 2]
            xnTg = xnTg_sb[g % 2]
            wts = [w for w in (
                ((cS, xg_store[g % 2]) if xg_store[g % 2] else None),
                xg_busy[g % 2]) if w]
            bld.op("sync", lambda e, xg=xg, g=g: e.dma_start(
                out=xg[:, :].rearrange("p (dc t) -> p dc t", t=512),
                in_=x_src.rearrange("(dc p) t -> p dc t", p=128)[:, :, g * 512:(g + 1) * 512]),
                inc=cA, incn=16, waits=wts)
            v_xg = cA.v
            wts = [(cS, xnT_store_g[g]), xnTg_busy[g % 2]]
            bld.op("sync", lambda e, xnTg=xnTg, g=g: e.dma_start(
                out=xnTg[:, :].rearrange("p (dc t) -> p dc t", t=512),
                in_=xn_sc.rearrange("(dc p) t -> p dc t", p=128)[:, :, g * 512:(g + 1) * 512]),
                inc=cA, incn=16, waits=wts)
            v_xnTg = cA.v
            for e_i in range(NE):
                wts = [(cV, comb_done_g[g])]
                if bcast_free is not None:
                    wts.append((cV, bcast_free))
                wts.append((cA, init_done["a"]))
                bld.op("tensor", lambda t_, e_i=e_i, g=g: t_.matmul(
                    ps_bc[:, :], sel_sb[0:8, e_i * 128:(e_i + 1) * 128],
                    combT_sb[0:8, g * 512:(g + 1) * 512],
                    start=True, stop=True), inc=cP, waits=wts)
                v_bc = cP.v
                v_w1_cur = v_w1
                for hc in range(NHC):
                    pbank = ps_h[hc % 2]
                    wts = [(cW, v_w1_cur), (cA, v_xnTg)]
                    if (hc % 2) in silu_done:
                        wts.append((cC, silu_done[hc % 2]))
                    for dc in range(NDC):
                        bld.op("tensor", lambda t_, pbank=pbank, dc=dc, hc=hc, xnTg=xnTg: t_.matmul(
                            pbank[:, :],
                            w1_sb[:, dc * HD + hc * 128: dc * HD + (hc + 1) * 128],
                            xnTg[:, dc * 512:(dc + 1) * 512],
                            start=(dc == 0), stop=(dc == NDC - 1)),
                            inc=(cP if dc == NDC - 1 else None), waits=wts)
                        wts = []
                    v_f1 = cP.v
                    bld.op("scalar", lambda s, pbank=pbank, hc=hc, e_i=e_i: s.activation(
                        h_sb[:, hc * 512:(hc + 1) * 512], pbank[:, :], AF.Silu,
                        bias=b1c_sb[:, e_i * NHC + hc: e_i * NHC + hc + 1]),
                        inc=cC, waits=[(cP, v_f1), mult_done.get(hc)])
                    silu_done[hc % 2] = cC.v
                    bld.op("vector", lambda v, hc=hc: v.tensor_tensor(
                        h_sb[:, hc * 512:(hc + 1) * 512],
                        h_sb[:, hc * 512:(hc + 1) * 512], ps_bc[:, :], ALU.mult),
                        inc=cV, waits=[(cC, silu_done[hc % 2]), (cP, v_bc)])
                    mult_done[hc] = (cV, cV.v)
                v_f1_last = cP.v
                nxt = e_i + 1
                if not (e_i == NE - 1 and g == NG - 1):
                    v_w1 = load_w1(nxt % NE, [(cP, v_f1_last)])
                bcast_free = mult_done[NHC - 1][1]
                v_w2_cur = v_w2
                for doc in range(NDC):
                    pbank = ps_y[doc % 2]
                    wts = [(cW, v_w2_cur)]
                    if (doc % 2) in evict_done:
                        wts.append((cV, evict_done[doc % 2]))
                    first = True
                    if e_i == 0:
                        bld.op("tensor", lambda t_, pbank=pbank, doc=doc, g=g: t_.matmul(
                            pbank[:, :], b2_sb[0:8, doc * 128:(doc + 1) * 128],
                            combT_sb[0:8, g * 512:(g + 1) * 512],
                            start=True, stop=False),
                            waits=wts + [(cV, comb_done_g[g]), (cW, init_done["w"])])
                        wts = []
                        first = False
                    for hc in range(NHC):
                        wts2 = wts + [mult_done[hc]]
                        wts = []
                        bld.op("tensor", lambda t_, pbank=pbank, doc=doc, hc=hc, first=first: t_.matmul(
                            pbank[:, :],
                            w2_sb[:, hc * D + doc * 128: hc * D + (doc + 1) * 128],
                            h_sb[:, hc * 512:(hc + 1) * 512],
                            start=(first and hc == 0), stop=(hc == NHC - 1)),
                            inc=(cP if hc == NHC - 1 else None), waits=wts2)
                    v_f2 = cP.v
                    bld.op("vector", lambda v, xg=xg, doc=doc, pbank=pbank: v.tensor_tensor(
                        xg[:, doc * 512:(doc + 1) * 512], pbank[:, :],
                        xg[:, doc * 512:(doc + 1) * 512], ALU.add),
                        inc=cV, waits=[(cP, v_f2), (cA, v_xg)])
                    evict_done[doc % 2] = cV.v
                v_f2_last = cP.v
                if not (e_i == NE - 1 and g == NG - 1):
                    v_w2 = load_w2(nxt % NE, [(cP, v_f2_last)])
            xnTg_busy[g % 2] = (cP, v_f1_last)
            v_all_ev = cV.v
            xg_busy[g % 2] = (cV, v_all_ev)
            bld.op("sync", lambda e, xg=xg, g=g: e.dma_start(
                out=x_dst.rearrange("(dc p) t -> p dc t", p=128)[:, :, g * 512:(g + 1) * 512],
                in_=xg[:, :].rearrange("p (dc t) -> p dc t", t=512)),
                inc=cS, incn=16, waits=[(cV, v_all_ev)])
            xg_store[g % 2] = cS.v
        ffn_state["w1_free"] = (cP, v_f1_last)
        ffn_state["w2_free"] = (cP, v_f2_last)
        pb_s.close()
        call.close()

    def emit_attention(x_src, x_dst):
        join_all()
        drain_barrier(60)
        attn = ExitStack()
        kT_full = sb(attn, "kT_full", [128, 4096], BF16)
        v_aug = sb(attn, "v_aug", [128, NKB * 132], BF16)
        qt_scope = ExitStack()
        bld.uid += 1
        qT = qt_scope.enter_context(
            nc.sbuf_tensor(f"qT_{bld.uid}", [128, NH * NT], BF16, side="right"))
        ln_scope = ExitStack()
        bld.uid += 1
        ln2T = ln_scope.enter_context(
            nc.sbuf_tensor(f"ln2T_{bld.uid}", [128, NDC * NT], BF16, side="right"))
        if dbg2.get("xload3d") is not None:
            tdbg = 4
            xl_scope = ExitStack()
            xl3 = sb(xl_scope, "xl3", [128, NDC * 128], F32)
            xl2 = sb(xl_scope, "xl2", [128, 2 * 128], F32)
            bld.op("sync", lambda e: e.dma_start(
                out=xl3[:, :].rearrange("p (dc t) -> p dc t", t=128),
                in_=x_src.rearrange("(dc p) t -> p dc t", p=128)[:, :, tdbg * 128:(tdbg + 1) * 128]),
                inc=cA, incn=16)
            bld.op("sync", lambda e: e.dma_start(
                out=xl2[:, 0:128], in_=x_src[13 * 128:14 * 128, tdbg * 128:(tdbg + 1) * 128]),
                inc=cA, incn=16)
            bld.op("sync", lambda e: e.dma_start(
                out=xl2[:, 128:256], in_=x_src[14 * 128:15 * 128, tdbg * 128:(tdbg + 1) * 128]),
                inc=cA, incn=16)
            bld.op("sync", lambda e: e.dma_start(out=dbg2["xload3d"], in_=xl3[:, :]),
                   inc=cS, incn=16, waits=[(cA, cA.v)])
            bld.op("sync", lambda e: e.dma_start(out=dbg2["xload2d"], in_=xl2[:, :]),
                   inc=cS, incn=16)
            bld.wait("sync", [(cS, cS.v)])
            xl_scope.close()
        c1 = ExitStack()
        ones_colb = sb(c1, "ones_colb", [128, 128], BF16)
        xg_c1 = sb(c1, "xg_c1", [128, NDC * 512], F32)
        xb_c1 = [sb(c1, "xb_c1", [128, 512], BF16) for _ in range(2)]
        xsq_c1 = [sb(c1, "xsq_c1", [128, 512], BF16) for _ in range(2)]
        t1_c1 = sb(c1, "t1_c1", [128, 512], F32)
        rows_c1 = sb(c1, "rows_c1", [1, 4 * 512], F32)
        rowsb_c1 = sb(c1, "rowsb_c1", [1, 2 * 512], BF16)
        ps_m = ps(c1, "ps_m", [128, 512], F32)
        ps_sq = ps(c1, "ps_sq", [128, 512], F32)
        ps_mb = ps(c1, "ps_mb", [128, 512], F32)
        ps_rb = ps(c1, "ps_rb", [128, 512], F32)
        bld.op("gpsimd", lambda g: g.memset(ones_colb[:, :], 1.0), inc=cG)
        v_onesf = cG.v
        v_lnw = None
        for g in range(NG):
            bld.op("sync", lambda e, g=g: e.dma_start(
                out=xg_c1[:, :].rearrange("p (dc t) -> p dc t", t=512),
                in_=x_src.rearrange("(dc p) t -> p dc t", p=128)[:, :, g * 512:(g + 1) * 512]),
                inc=cA, incn=16, waits=[((cV, v_lnw) if v_lnw else None)])
            v_xgl = cA.v
            cast_done = {}
            for dc in range(NDC):
                bld.op("scalar", lambda s, dc=dc: s.copy(
                    xb_c1[dc % 2][:, :], xg_c1[:, dc * 512:(dc + 1) * 512]),
                    inc=cC, waits=[(cA, v_xgl), cast_done.get(("b", dc % 2))])
                bld.op("scalar", lambda s, dc=dc: s.activation(
                    xsq_c1[dc % 2][:, :], xg_c1[:, dc * 512:(dc + 1) * 512], AF.Square),
                    inc=cC, waits=[cast_done.get(("q", dc % 2))])
                v_sq = cC.v
                bld.op("tensor", lambda t_, dc=dc: t_.matmul(
                    ps_m[:, :], ones_colb[:, :], xb_c1[dc % 2][:, :],
                    start=(dc == 0), stop=(dc == NDC - 1)),
                    inc=cP, waits=[(cC, v_sq), (cG, v_onesf),
                                   ((cV, v_lnw) if (dc == 0 and v_lnw) else None)])
                cast_done[("b", dc % 2)] = (cP, cP.v)
                bld.op("tensor", lambda t_, dc=dc: t_.matmul(
                    ps_sq[:, :], ones_colb[:, :], xsq_c1[dc % 2][:, :],
                    start=(dc == 0), stop=(dc == NDC - 1)),
                    inc=cP)
                cast_done[("q", dc % 2)] = (cP, cP.v)
            v_stats = cP.v
            bld.op("vector", lambda v: v.tensor_scalar(
                rows_c1[0:1, 0:512], ps_m[0:1, :], -1.0 / D, 0.0, ALU.mult, ALU.add),
                inc=cV, waits=[(cP, v_stats)])
            bld.op("vector", lambda v: v.tensor_mul(
                rows_c1[0:1, 512:1024], rows_c1[0:1, 0:512], rows_c1[0:1, 0:512]),
                inc=cV, waits=[(cV, cV.v)])
            bld.op("vector", lambda v: v.scalar_tensor_tensor(
                rows_c1[0:1, 1024:1536], ps_sq[0:1, :], 1.0 / D, rows_c1[0:1, 512:1024],
                op0=ALU.mult, op1=ALU.subtract), inc=cV, waits=[(cV, cV.v)])
            v_var = cV.v
            bld.op("scalar", lambda s: s.activation(
                rows_c1[0:1, 1536:2048], rows_c1[0:1, 1024:1536], AF.Ln,
                bias=eps_col[0:1, :]), inc=cC, waits=[(cV, v_var)])
            bld.op("scalar", lambda s: s.activation(
                rows_c1[0:1, 1536:2048], rows_c1[0:1, 1536:2048], AF.Exp,
                scale=-0.5), inc=cC, waits=[(cC, cC.v)])
            v_rstd = cC.v
            bld.op("vector", lambda v: v.tensor_copy(
                rowsb_c1[0:1, 0:512], rows_c1[0:1, 0:512]), inc=cV,
                waits=[(cC, v_rstd)])
            bld.op("vector", lambda v: v.tensor_copy(
                rowsb_c1[0:1, 512:1024], rows_c1[0:1, 1536:2048]), inc=cV,
                waits=[(cV, cV.v)])
            v_rows = cV.v
            bld.op("tensor", lambda t_: t_.matmul(
                ps_mb[:, :], ones_b[:, :], rowsb_c1[0:1, 0:512],
                start=True, stop=True), waits=[(cV, v_rows)])
            bld.op("tensor", lambda t_: t_.matmul(
                ps_rb[:, :], ones_b[:, :], rowsb_c1[0:1, 512:1024],
                start=True, stop=True), inc=cP)
            v_bc = cP.v
            v_last = None
            for dc in range(NDC):
                bld.op("vector", lambda v, dc=dc: v.tensor_tensor(
                    t1_c1[:, :], xg_c1[:, dc * 512:(dc + 1) * 512], ps_mb[:, :],
                    ALU.add), inc=cV,
                    waits=[(cP, v_bc)] + ([(cV, v_last)] if v_last else []))
                bld.op("vector", lambda v, dc=dc, g=g: v.tensor_tensor(
                    ln2T[:, dc * NT + g * 512: dc * NT + (g + 1) * 512],
                    t1_c1[:, :], ps_rb[:, :], ALU.mult),
                    inc=cV, waits=[(cV, cV.v)])
                v_last = cV.v
            v_lnw = cV.v
            if g == 0 and dbg2.get("rows") is not None:
                bld.op("sync", lambda e: e.dma_start(out=dbg2["rows"], in_=rows_c1[0:1, :]),
                       inc=cS, incn=16, waits=[(cV, v_lnw)])
                psm_sb = t1_c1
                bld.op("vector", lambda v: v.tensor_copy(psm_sb[0:1, 0:512], ps_m[0:1, :]),
                       inc=cV, waits=[(cV, v_lnw)])
                bld.op("sync", lambda e: e.dma_start(out=dbg2["psm"][0:1, 0:512],
                                                     in_=psm_sb[0:1, 0:512]),
                       inc=cS, incn=16, waits=[(cV, cV.v)])
                bld.op("vector", lambda v: v.tensor_copy(psm_sb[0:1, 0:512], ps_sq[0:1, :]),
                       inc=cV, waits=[(cS, cS.v)])
                bld.op("sync", lambda e: e.dma_start(out=dbg2["psm"][0:1, 512:1024],
                                                     in_=psm_sb[0:1, 0:512]),
                       inc=cS, incn=16, waits=[(cV, cV.v)])
                bld.wait("vector", [(cS, cS.v)])
        c1.close()
        join_all()
        v_ln2T_v = cV.v
        ln_done = cC.v
        bld.wait("tensor", [(cV, v_ln2T_v)])
        bld.wait("scalar", [(cV, v_ln2T_v)])

        c2ps = ExitStack()
        ps_kv = ps(c2ps, "ps_kv", [128, 512], F32)
        ps_vt = [ps(c2ps, "ps_vt", [128, 1024], BF16) for _ in range(2)]

        c2a = ExitStack()
        wkv_sb = sb(c2a, "wkv", [128, NDC * 256], BF16)
        ktloc = sb(c2a, "ktloc", [128, NT], BF16)
        vtloc = sb(c2a, "vtloc", [128, NT], BF16)
        bld.op("gpsimd", lambda g: g.dma_start(
            out=wkv_sb[:, 0:2048], in_=wqkvT[16]), inc=cW, incn=16)
        bld.op("gpsimd", lambda g: g.dma_start(
            out=wkv_sb[:, 2048:4096], in_=wqkvT[17]), inc=cW, incn=16)
        v_wkv = cW.v
        kv_ev = None
        for kvb, dst in ((0, ktloc), (1, vtloc)):
            for g in range(NG):
                wts = [(cW, v_wkv), (cC, ln_done), ((cC, kv_ev) if kv_ev else None)]
                for dc in range(NDC):
                    bld.op("tensor", lambda t_, dc=dc, kvb=kvb, g=g: t_.matmul(
                        ps_kv[:, :],
                        wkv_sb[:, kvb * 2048 + dc * 128: kvb * 2048 + (dc + 1) * 128],
                        ln2T[:, dc * NT + g * 512: dc * NT + (g + 1) * 512],
                        start=(dc == 0), stop=(dc == NDC - 1)),
                        inc=(cP if dc == NDC - 1 else None), waits=wts)
                    wts = []
                v_mm = cP.v
                bld.op("scalar", lambda s, dst=dst, g=g, kvb=kvb: s.activation(
                    dst[:, g * 512:(g + 1) * 512], ps_kv[:, :], AF.Identity,
                    bias=qkvb_sb[:, 16 + kvb:17 + kvb]), inc=cC, waits=[(cP, v_mm)])
                kv_ev = cC.v
        bld.op("sync", lambda e: e.dma_start(out=cc_in[0:128, :], in_=ktloc[:, :]),
               inc=cS, incn=16, waits=[(cC, kv_ev)])
        bld.op("sync", lambda e: e.dma_start(out=cc_in[128:256, :], in_=vtloc[:, :]),
               inc=cS, incn=16)
        v_pub = cS.v
        bld.op("gpsimd", lambda g: g.collective_compute(
            "AllGather", ALU.bypass,
            replica_groups=[[0, 1], [2, 3], [4, 5], [6, 7]],
            ins=[cc_in.opt()], outs=[cc_out.opt()]),
            inc=cCC, waits=[(cS, v_pub)])
        v_cc = cCC.v
        c2a.close()

        c2b = ExitStack()
        vT_full = sb(c2b, "vT_full", [128, 4096], BF16)
        wq_sb = [sb(c2b, "wq", [128, NDC * 128], BF16) for _ in range(2)]
        q_done_h = {}
        v_q_ev = None
        for h in range(NH):
            wts = [q_done_h.get(h - 2), (cS, v_pub)]
            v_wq = bld.op("gpsimd", lambda g, h=h: g.dma_start(
                out=wq_sb[h % 2][:, :], in_=wqkvT[h]), inc=cW, incn=16, waits=wts)
            for g in range(NG):
                wts = [(cW, v_wq), (cC, ln_done), ((cC, v_q_ev) if v_q_ev else None)]
                for dc in range(NDC):
                    bld.op("tensor", lambda t_, h=h, g=g, dc=dc: t_.matmul(
                        ps_kv[:, :],
                        wq_sb[h % 2][:, dc * 128:(dc + 1) * 128],
                        ln2T[:, dc * NT + g * 512: dc * NT + (g + 1) * 512],
                        start=(dc == 0), stop=(dc == NDC - 1)),
                        inc=(cP if dc == NDC - 1 else None), waits=wts)
                    wts = []
                v_mm = cP.v
                bld.op("scalar", lambda s, h=h, g=g: s.activation(
                    qT[:, h * NT + g * 512: h * NT + (g + 1) * 512], ps_kv[:, :],
                    AF.Identity, bias=qkvb_sb[:, h:h + 1]), inc=cC, waits=[(cP, v_mm)])
                v_q_ev = cC.v
            q_done_h[h] = (cP, cP.v)
        v_q_done = cC.v
        q_mm_done = cP.v

        bld.wait("sync", [(cCC, v_cc)])
        for r, rows in ((0, ROWS_EVEN), (1, ROWS_ODD)):
            for si in range(8):
                grow = rows[si]
                bld.op("sync", lambda e, r=r, si=si, grow=grow: e.dma_start(
                    out=kT_full[:, grow * 256:(grow + 1) * 256],
                    in_=cc_out[r * 256: r * 256 + 128, si * 256:(si + 1) * 256]),
                    inc=cA, incn=16)
                bld.op("sync", lambda e, r=r, si=si, grow=grow: e.dma_start(
                    out=vT_full[:, grow * 256:(grow + 1) * 256],
                    in_=cc_out[r * 256 + 128: r * 256 + 256, si * 256:(si + 1) * 256]),
                    inc=cA, incn=16)
        v_kvfull = cA.v
        bld.op("gpsimd", lambda g: g.memset(v_aug[:, :], 1.0), inc=cG)
        v_vaug_ms = cG.v
        vt_ev = {}
        for kb in range(NKB):
            wts = [(cA, v_kvfull), (cG, v_vaug_ms), vt_ev.get(kb % 2)]
            bld.op("tensor", lambda t_, kb=kb: t_.transpose(
                ps_vt[kb % 2][:, 0:128], vT_full[:, kb * 128:(kb + 1) * 128],
                ident_b[:, :]), inc=cP, waits=wts)
            v_t = cP.v
            bld.op("vector", lambda v, kb=kb: v.tensor_copy(
                v_aug[:, kb * 132: kb * 132 + 128], ps_vt[kb % 2][:, 0:128]),
                inc=cV, waits=[(cP, v_t)])
            vt_ev[kb % 2] = (cV, cV.v)
        v_vaug = cV.v
        flush_sc = nc.dram_tensor(f"flush_sc_{bld.uid}", [128, 2048], F32).ap()
        flush_sb = sb(c2b, "flush_sb", [128, 2048], F32)

        def flush(src_sb, ncols, wait):
            nch = (ncols + 2047) // 2048
            for ch in range(nch):
                c0 = ch * 2048
                c1 = min(ncols, c0 + 2048)
                bld.op("vector", lambda v, c0=c0, c1=c1, src_sb=src_sb: v.tensor_copy(
                    flush_sb[:, 0:c1 - c0], src_sb[:, c0:c1]),
                    inc=cV, waits=[wait, (cS, cS.v)])
                bld.op("sync", lambda e, c0=c0, c1=c1: e.dma_start(
                    out=flush_sc[:, 0:c1 - c0], in_=flush_sb[:, 0:c1 - c0]),
                    inc=cS, incn=16, waits=[(cV, cV.v)])
            bld.wait("vector", [(cS, cS.v)])

        flush(kT_full, 4096, (cA, v_kvfull))
        flush(v_aug, NKB * 132, (cV, v_vaug))
        flush(qT, 2 * NT, (cC, v_q_done))
        if dbg2.get("ln2T_post") is not None:
            nch = (NDC * NT + 2047) // 2048
            for ch in range(nch):
                bld.op("vector", lambda v, ch=ch: v.tensor_copy(
                    flush_sb[:, :], ln2T[:, ch * 2048:(ch + 1) * 2048]),
                    inc=cV, waits=[(cS, cS.v), (cC, ln_done)])
                bld.op("sync", lambda e, ch=ch: e.dma_start(
                    out=dbg2["ln2T_post"][:, ch * 2048:(ch + 1) * 2048],
                    in_=flush_sb[:, :]), inc=cS, incn=16, waits=[(cV, cV.v)])
            bld.wait("vector", [(cS, cS.v)])
        if dbg:
            dbg_sb = sb(c2b, "dbgsb", [128, 2048], F32)

            def dump(src_sb, dst_ap, ncols, wait):
                nch = (ncols + 2047) // 2048
                for ch in range(nch):
                    c0 = ch * 2048
                    c1 = min(ncols, c0 + 2048)
                    bld.op("vector", lambda v, c0=c0, c1=c1, src_sb=src_sb: v.tensor_copy(
                        dbg_sb[:, 0:c1 - c0], src_sb[:, c0:c1]),
                        inc=cV, waits=[wait, (cS, cS.v)])
                    bld.op("sync", lambda e, c0=c0, c1=c1, dst_ap=dst_ap: e.dma_start(
                        out=dst_ap[:, c0:c1], in_=dbg_sb[:, 0:c1 - c0]),
                        inc=cS, incn=16, waits=[(cV, cV.v)])
                bld.wait("vector", [(cS, cS.v)])

            if "kT" in dbg:
                dump(kT_full, dbg["kT"], 4096, (cA, v_kvfull))
            if "vaug" in dbg:
                dump(v_aug, dbg["vaug"], NKB * 132, (cV, v_vaug))
            if "ln2T" in dbg:
                dump(ln2T, dbg["ln2T"], NDC * NT, (cC, ln_done))
            if "qT" in dbg:
                dump(qT, dbg["qT"], NH * NT, (cC, v_q_done))
        c2b.close()
        c2ps.close()
        ln_scope.close()

        post = ExitStack()
        ctxT = sb(post, "ctxT", [128, NH * NT], BF16)
        masks_sb = sb(post, "masks_sb", [128, 8 * 4 * 256], BF16)
        bld.op("sync", lambda e: e.dma_start(
            out=masks_sb[:, :], in_=masks_d), inc=cA, incn=16)
        v_masks = cA.v
        c5 = ExitStack()
        expT = sb(c5, "expT", [128, 2 * 512], BF16)
        msk_sb = sb(c5, "msk", [128, 4 * 256], F32)
        ctxn = sb(c5, "ctxn", [128, 2 * 128], BF16)
        rc_sb = sb(c5, "rc", [128, 2], F32)
        ps_s = [ps(c5, "ps_s", [128, 512], F32) for _ in range(2)]
        ps_cx = [ps(c5, "ps_cx", [128, 512], F32) for _ in range(2)]
        ps_ct2 = [ps(c5, "ps_ct2", [128, 1024], BF16) for _ in range(2)]
        join_all()
        exp_done = {}
        expT_free = {}
        mask_read = {}
        msk_free = {}
        ctx_done = {}
        ctxn_free = {}
        ctxT_ev = {}
        import os as _os2
        _join_lvl = int(_os2.environ.get("MK_C5_JOIN", "0"))
        _nh_c5 = int(_os2.environ.get("MK_NH_C5", str(NH)))
        for h in range(_nh_c5):
            if _join_lvl >= 1:
                join_all()
            for s_i in range(8):
                if _join_lvl >= 2:
                    join_all()
                L = L_PROG[s_i]
                qoff = h * NT + s_i * 256
                for ktc in range(0, L, 2):
                    pj = (ktc // 2) % 2
                    for half in range(2):
                        k2 = ktc + half
                        wts = [(cA, v_kvfull), (cC, v_q_done)]
                        if half == 0:
                            wts.append(exp_done.get(pj))
                            wts.append(mask_read.get(pj))
                        bld.op("tensor", lambda t_, pj=pj, half=half, k2=k2, qoff=qoff: t_.matmul(
                            ps_s[pj][:, half * 256:(half + 1) * 256],
                            kT_full[:, k2 * 128:(k2 + 1) * 128],
                            qT[:, qoff: qoff + 256],
                            start=True, stop=True),
                            inc=(cP if half == 1 else None), waits=wts)
                    v_smm = cP.v
                    in_zone = [ktc >= L - 4, ktc + 1 >= L - 4]
                    if _os2.environ.get("MK_NO_MASK"):
                        in_zone = [False, False]
                    if any(in_zone):
                        for half in range(2):
                            if not in_zone[half]:
                                continue
                            j = ktc + half - (L - 4)
                            moff = (s_i * 4 + j) * 256
                            bld.op("vector", lambda v, pj=pj, half=half, moff=moff: v.tensor_tensor(
                                msk_sb[:, (pj * 2 + half) * 256:(pj * 2 + half + 1) * 256],
                                ps_s[pj][:, half * 256:(half + 1) * 256],
                                masks_sb[:, moff:moff + 256],
                                ALU.add), inc=cV,
                                waits=[(cP, v_smm), (cA, v_masks), exp_done.get(pj),
                                       msk_free.get(pj * 2 + half)])
                            mask_read[pj] = (cV, cV.v)
                        for half in range(2):
                            wts_exp = [expT_free.get(pj)]
                            if in_zone[half]:
                                src_ap = msk_sb[:, (pj * 2 + half) * 256:(pj * 2 + half + 1) * 256]
                                wts_exp.append(mask_read[pj])
                            else:
                                src_ap = ps_s[pj][:, half * 256:(half + 1) * 256]
                                wts_exp.append((cP, v_smm))
                            bld.op("scalar", lambda s, pj=pj, half=half, src_ap=src_ap: s.activation(
                                expT[:, pj * 512 + half * 256: pj * 512 + (half + 1) * 256],
                                src_ap, AF.Exp, scale=INV_SQRT_HD),
                                inc=cC, waits=wts_exp)
                            if in_zone[half]:
                                msk_free[pj * 2 + half] = (cC, cC.v)
                    else:
                        bld.op("scalar", lambda s, pj=pj: s.activation(
                            expT[:, pj * 512:(pj + 1) * 512], ps_s[pj][:, :],
                            AF.Exp, scale=INV_SQRT_HD),
                            inc=cC, waits=[(cP, v_smm), expT_free.get(pj)])
                    exp_done[pj] = (cC, cC.v)
                    if _os2.environ.get("MK_SKIP_CTX"):
                        expT_free[pj] = exp_done[pj][0], 0
                        expT_free[pj] = None
                        continue
                    for half in range(2):
                        k2 = ktc + half
                        for qsub in range(2):
                            wts = [exp_done[pj]]
                            if k2 == 0:
                                wts.append(ctx_done.get(qsub))
                                wts.append((cV, v_vaug))
                            last_pair = (half == 1 and qsub == 1)
                            bld.op("tensor", lambda t_, pj=pj, half=half, qsub=qsub, k2=k2: t_.matmul(
                                ps_cx[qsub][:, 0:129],
                                expT[:, pj * 512 + half * 256 + qsub * 128:
                                     pj * 512 + half * 256 + (qsub + 1) * 128],
                                v_aug[:, k2 * 132: k2 * 132 + 129],
                                start=(k2 == 0), stop=(k2 == L - 1)),
                                inc=(cP if last_pair else None),
                                waits=wts)
                    expT_free[pj] = (cP, cP.v)
                v_cx = cP.v
                if _os2.environ.get("MK_SKIP_CTX"):
                    continue
                for qsub in range(2):
                    bld.op("vector", lambda v, qsub=qsub: v.reciprocal(
                        rc_sb[:, qsub:qsub + 1], ps_cx[qsub][:, 128:129]),
                        inc=cV, waits=[(cP, v_cx), ctxn_free.get(qsub)])
                    bld.op("vector", lambda v, qsub=qsub: v.tensor_scalar(
                        ctxn[:, qsub * 128:(qsub + 1) * 128], ps_cx[qsub][:, 0:128],
                        rc_sb[:, qsub:qsub + 1], 0.0, ALU.mult, ALU.add),
                        inc=cV, waits=[(cV, cV.v)])
                    ctx_done[qsub] = (cV, cV.v)
                    wts = [ctx_done[qsub], ctxT_ev.get(qsub)]
                    bld.op("tensor", lambda t_, qsub=qsub: t_.transpose(
                        ps_ct2[qsub][:, 0:128], ctxn[:, qsub * 128:(qsub + 1) * 128],
                        ident_b[:, :]), inc=cP, waits=wts)
                    ctxn_free[qsub] = (cP, cP.v)
                    v_t = cP.v
                    if _os2.environ.get("MK_CTX_SCRATCH"):
                        bld.op("vector", lambda v, qsub=qsub: v.tensor_copy(
                            ctxn[:, qsub * 128:(qsub + 1) * 128] if False else msk_sb[:, 0:128].bitcast(BF16)[:, 0:256][:, 0:128],
                            ps_ct2[qsub][:, 0:128]), inc=cV, waits=[(cP, v_t)])
                    else:
                        bld.op("vector", lambda v, qoff=qoff, qsub=qsub: v.tensor_copy(
                            ctxT[:, qoff + qsub * 128: qoff + (qsub + 1) * 128],
                            ps_ct2[qsub][:, 0:128]), inc=cV, waits=[(cP, v_t)])
                    ctxT_ev[qsub] = (cV, cV.v)
        v_ctxT = cV.v
        if dbg2.get("ctxT") is not None:
            dbg_sb5 = sb(c5, "dbgctx2", [128, 2048], F32)

            def dump2(src_sb, dst_ap, ncols):
                nch = (ncols + 2047) // 2048
                for ch in range(nch):
                    c0 = ch * 2048
                    c1 = min(ncols, c0 + 2048)
                    bld.op("vector", lambda v, c0=c0, c1=c1, src_sb=src_sb: v.tensor_copy(
                        dbg_sb5[:, 0:c1 - c0], src_sb[:, c0:c1]),
                        inc=cV, waits=[(cS, cS.v), (cV, v_ctxT)])
                    bld.op("sync", lambda e, c0=c0, c1=c1, dst_ap=dst_ap: e.dma_start(
                        out=dst_ap[:, c0:c1], in_=dbg_sb5[:, 0:c1 - c0]),
                        inc=cS, incn=16, waits=[(cV, cV.v)])
                    bld.wait("vector", [(cS, cS.v)])

            dump2(ctxT, dbg2["ctxT"], NH * NT)
            if "qT_post" in dbg2:
                dump2(qT, dbg2["qT_post"], NH * NT)
                dump2(kT_full, dbg2["kT_post"], 4096)
                dump2(v_aug, dbg2["vaug_post"], NKB * 132)
        if dbg.get("ctxT") is not None:
            dbg_sb4 = sb(c5, "dbgctx", [128, 2048], F32)
            for ch in range(NH):
                bld.op("vector", lambda v, ch=ch: v.tensor_copy(
                    dbg_sb4[:, :], ctxT[:, ch * NT:(ch + 1) * NT]),
                    inc=cV, waits=[(cS, cS.v), (cV, v_ctxT)])
                bld.op("sync", lambda e, ch=ch: e.dma_start(
                    out=dbg["ctxT"].rearrange("p (c f) -> p c f", f=NT)[:, ch, :],
                    in_=dbg_sb4[:, :]), inc=cS, incn=16, waits=[(cV, cV.v)])
                bld.wait("vector", [(cS, cS.v)])
        c5.close()
        qt_scope.close()

        c6 = ExitStack()
        wo_sb = [sb(c6, "wo", [128, NDC * 128], BF16) for _ in range(2)]
        x2dc = [sb(c6, "x2dc", [128, NT], F32) for _ in range(2)]
        x3st = [sb(c6, "x3st", [128, NT], F32) for _ in range(2)]
        ps_o = ps(c6, "ps_o", [128, 2048], F32)
        join_all()
        v_wo = [None, None]
        v_x2 = [None, None]
        v_x3ev = [None, None]
        v_x3st = [None, None]
        prev_ev = None
        for doc in range(NDC):
            v_wo[doc % 2] = bld.op("gpsimd", lambda g, doc=doc: g.dma_start(
                out=wo_sb[doc % 2][:, :], in_=woT[doc]), inc=cW, incn=16,
                waits=[((cV, v_x3ev[doc % 2]) if v_x3ev[doc % 2] else None)])
            v_x2[doc % 2] = bld.op("sync", lambda e, doc=doc: e.dma_start(
                out=x2dc[doc % 2][:, :],
                in_=x_src[doc * 128:(doc + 1) * 128, :]), inc=cA, incn=16,
                waits=[((cS, v_x3st[doc % 2]) if v_x3st[doc % 2] else None)])
            wts = [(cW, v_wo[doc % 2]), (cV, v_ctxT)]
            if prev_ev is not None:
                wts.append((cV, prev_ev))
            for cdc in range(NDC):
                for qb in range(4):
                    bld.op("tensor", lambda t_, doc=doc, cdc=cdc, qb=qb: t_.matmul(
                        ps_o[:, qb * 512:(qb + 1) * 512],
                        wo_sb[doc % 2][:, cdc * 128:(cdc + 1) * 128],
                        ctxT[:, cdc * NT + qb * 512:cdc * NT + (qb + 1) * 512],
                        start=(cdc == 0), stop=(cdc == NDC - 1)),
                        inc=(cP if (cdc == NDC - 1 and qb == 3) else None), waits=wts)
                    wts = []
            v_omm = cP.v
            bld.op("vector", lambda v, doc=doc: v.scalar_tensor_tensor(
                x3st[doc % 2][:, :], ps_o[:, :], outb_sb[:, doc:doc + 1],
                x2dc[doc % 2][:, :], op0=ALU.add, op1=ALU.add),
                inc=cV, waits=[(cP, v_omm), (cA, v_x2[doc % 2]),
                               ((cS, v_x3st[doc % 2]) if v_x3st[doc % 2] else None)])
            v_x3ev[doc % 2] = cV.v
            prev_ev = cV.v
            bld.op("sync", lambda e, doc=doc: e.dma_start(
                out=x_dst[doc * 128:(doc + 1) * 128, :],
                in_=x3st[doc % 2][:, :]), inc=cS, incn=16, waits=[(cV, prev_ev)])
            v_x3st[doc % 2] = cS.v
        c6.close()
        post.close()
        attn.close()

    if mode == "ffn":
        emit_ffn_call(xT, y, "f1")
    elif mode.startswith("attn"):
        emit_attention(xT, y)
    else:
        emit_ffn_call(xT, x2_sc, "f1")
        emit_attention(x2_sc, x3_sc)
        emit_ffn_call(x3_sc, y, "f2")

    bld.wait("sync", [(cS, cS.v)])

    with nc.Block() as block:
        bld.emit(block)
    es.close()
    return nc



def _rows(core):
    return ROWS_EVEN if core % 2 == 0 else ROWS_ODD


def _token_idx(core):
    return np.concatenate([np.arange(r * 256, (r + 1) * 256) for r in _rows(core)])


def _make_masks(core):
    i = np.arange(128)[:, None]
    j = np.arange(256)[None, :]
    neg = np.float32(-1e6)
    m1 = np.where(i <= j, np.float32(0.0), neg)
    m2 = np.where((i + 128) <= j, np.float32(0.0), neg)
    free = np.zeros((128, 256), np.float32)
    full = np.full((128, 256), neg, np.float32)
    rows = _rows(core)
    out = np.empty((8, 4, 128, 256), np.float32)
    for s in range(8):
        l_true = 2 * rows[s] + 2
        tiles = [free, free, m1, m2] if l_true == L_PROG[s] else [m1, m2, full, full]
        for j4 in range(4):
            out[s, j4] = tiles[j4]
    return np.ascontiguousarray(
        out.transpose(2, 0, 1, 3).reshape(128, 8 * 4 * 256)).astype(ml_dtypes.bfloat16)


def _stage_inputs(L0_attn_x, L0_attn_Wqkv_w, L0_attn_Wqkv_b, L0_attn_out_w,
                  L0_attn_out_b, L0_ffn_gate_w, L0_ffn_gate_b, L0_ffn_bias,
                  L0_ffn_w1, L0_ffn_b1, L0_ffn_w2, L0_ffn_b2):
    x = np.asarray(L0_attn_x, np.float32)
    wqkvT_flat = np.asarray(L0_attn_Wqkv_w, np.float32).T
    wqkvT_tiled = np.ascontiguousarray(
        wqkvT_flat.reshape(16, 128, 2304).transpose(2, 1, 0).reshape(2304, 128, 16)
    ) if False else np.ascontiguousarray(
        np.stack([wqkvT_flat[:, ec * 128:(ec + 1) * 128]
                  .reshape(16, 128, 128).transpose(1, 0, 2).reshape(128, 2048)
                  for ec in range(18)])
    )
    woT_flat = np.asarray(L0_attn_out_w, np.float32).T
    woT_tiled = np.ascontiguousarray(
        np.stack([woT_flat[:, doc * 128:(doc + 1) * 128]
                  .reshape(16, 128, 128).transpose(1, 0, 2).reshape(128, 2048)
                  for doc in range(16)])
    )
    w1_arr = np.asarray(L0_ffn_w1, np.float32)
    w1_tiled = np.ascontiguousarray(
        np.stack([w1_arr[e].T.reshape(16, 128, 1408).transpose(1, 0, 2).reshape(128, 16 * 1408)
                  for e in range(8)])
    )
    w2_arr = np.asarray(L0_ffn_w2, np.float32)
    w2_tiled = np.ascontiguousarray(
        np.stack([w2_arr[e].T.reshape(11, 128, 2048).transpose(1, 0, 2).reshape(128, 11 * 2048)
                  for e in range(8)])
    )
    shared = {
        "wqkvT": wqkvT_tiled,
        "qkvb": np.ascontiguousarray(np.asarray(L0_attn_Wqkv_b, np.float32).reshape(18, 128).T),
        "woT": woT_tiled,
        "outb": np.ascontiguousarray(np.asarray(L0_attn_out_b, np.float32).reshape(16, 128).T),
        "gw": np.ascontiguousarray(np.asarray(L0_ffn_gate_w, np.float32)),
        "gb": np.ascontiguousarray(np.asarray(L0_ffn_gate_b, np.float32).reshape(1, 8)),
        "gbias": np.ascontiguousarray(np.asarray(L0_ffn_bias, np.float32).reshape(1, 8)),
        "w1t": w1_tiled,
        "b1c": np.ascontiguousarray(np.asarray(L0_ffn_b1, np.float32).reshape(8, 11, 128).transpose(2, 0, 1).reshape(128, 88)),
        "w2t": w2_tiled,
        "b2": np.ascontiguousarray(np.asarray(L0_ffn_b2, np.float32)),
        "sel": np.ascontiguousarray(
            np.eye(8, dtype=np.float32).repeat(128, axis=1).reshape(8, 8 * 128)
        ).astype(ml_dtypes.bfloat16),
    }
    in_maps = []
    for core in range(8):
        b = core // 2
        idx = _token_idx(core)
        m = dict(shared)
        m["xT"] = np.ascontiguousarray(x[b][idx].T)
        m["masks"] = _make_masks(core)
        in_maps.append(m)
    return in_maps


def kernel(**inputs):
    in_maps = _stage_inputs(**inputs)
    nc_ffn = build_program("ffn")
    nc_attn = build_program("attn")
    r1 = run_bass_kernel_spmd(nc_ffn, in_maps, core_ids=list(range(8)))
    maps2 = []
    for core in range(8):
        m = dict(in_maps[core])
        m["xT"] = r1.results[core]["y"]
        maps2.append(m)
    r2 = run_bass_kernel_spmd(nc_attn, maps2, core_ids=list(range(8)))
    maps3 = []
    for core in range(8):
        m = dict(in_maps[core])
        m["xT"] = r2.results[core]["y"]
        maps3.append(m)
    r3 = run_bass_kernel_spmd(nc_ffn, maps3, core_ids=list(range(8)))
    out = np.empty((4, 4096, 2048), np.float32)
    for core in range(8):
        b = core // 2
        idx = _token_idx(core)
        out[b][idx] = r3.results[core]["y"].T
    return out
